# version 03
# baseline (speedup 1.0000x reference)
import numpy as np
import concourse.bass as bass
import concourse.mybir as mybir
from concourse.bass_utils import run_bass_kernel_spmd

F32 = mybir.dt.float32
AF = mybir.ActivationFunctionType
ALU = mybir.AluOpType
AX = mybir.AxisListType.X

D = 2048
KT = 16
H = 16
DK = 128
TB = 512
CPB = 4
EPS = 1e-5
SCALE = DK ** -0.5
E = 8
I_FF = 1408
I_T = 11
SI_T = 22
NB = 4

DMA_ENGS = ("sp", "gp")

DBG = None
LAST_PLAN = None


class Buf:
    __slots__ = ("w", "r", "name")

    def __init__(self, name=""):
        self.w = []
        self.r = []
        self.name = name


class Step:
    __slots__ = ("eng", "fn", "deps", "val")


class Plan:
    def __init__(self):
        self.steps = {e: [] for e in ("pe", "dve", "act", "sp", "gp")}
        self.cnt = {e: 0 for e in ("pe", "dve", "act", "sp", "gp")}
        self.sems = {}

    def step(self, eng, fn, reads=(), writes=(), conflicts=(), dmas=1):
        deps = []
        for b in reads:
            deps += b.w
        for b in writes:
            deps += b.w + b.r
        for b in conflicts:
            deps += b.w + b.r
        s = Step()
        s.eng = eng
        s.fn = fn
        if eng in ("dve", "act"):
            s.deps = list(deps)
        else:
            s.deps = [d for d in deps if d.eng != eng]
        self.cnt[eng] += (16 * dmas) if eng in DMA_ENGS else 1
        s.val = self.cnt[eng]
        self.steps[eng].append(s)
        for b in reads:
            b.r.append(s)
        for b in writes:
            b.w = [s]
            b.r = []
        return s

    def run_engine(self, eng, builder):
        hwm = {}
        sem = self.sems[eng]
        for s in self.steps[eng]:
            need = {}
            for d in s.deps:
                if d.val > need.get(d.eng, 0):
                    need[d.eng] = d.val
            for k in ("pe", "dve", "act", "sp", "gp"):
                v = need.get(k)
                if v is None:
                    continue
                if k == eng:
                    if v < s.val - 8:
                        continue
                if hwm.get(k, 0) < v:
                    builder.wait_ge(self.sems[k], v)
                    hwm[k] = v
            last = s.fn(builder)
            if eng not in DMA_ENGS:
                last.then_inc(sem, 1)
        if eng == "gp":
            builder.wait_ge(sem, self.cnt["gp"])


def build_program():
    nc = bass.Bass("TRN2", target_bir_lowering=False)

    xT_h = nc.dram_tensor("xT", [D, 2048], F32, kind="ExternalInput")
    wqT_h = nc.dram_tensor("wqT", [D, D], F32, kind="ExternalInput")
    wkT_h = nc.dram_tensor("wkT", [D, D], F32, kind="ExternalInput")
    wvT_h = nc.dram_tensor("wvT", [D, D], F32, kind="ExternalInput")
    woT_h = nc.dram_tensor("woT", [D, D], F32, kind="ExternalInput")
    fq1T = nc.dram_tensor("fq1T", [DK, DK], F32, kind="ExternalInput").ap()
    fq2T = nc.dram_tensor("fq2T", [DK, DK], F32, kind="ExternalInput").ap()
    fk1T = nc.dram_tensor("fk1T", [DK, DK], F32, kind="ExternalInput").ap()
    fk2T = nc.dram_tensor("fk2T", [DK, DK], F32, kind="ExternalInput").ap()
    fq1b = nc.dram_tensor("fq1b", [DK, 1], F32, kind="ExternalInput").ap()
    fq2b = nc.dram_tensor("fq2b", [DK, 1], F32, kind="ExternalInput").ap()
    fk1b = nc.dram_tensor("fk1b", [DK, 1], F32, kind="ExternalInput").ap()
    fk2b = nc.dram_tensor("fk2b", [DK, 1], F32, kind="ExternalInput").ap()
    normw = nc.dram_tensor("normw", [DK, 1], F32, kind="ExternalInput").ap()
    gwT_h = nc.dram_tensor("gwT", [D, E], F32, kind="ExternalInput")
    w1_h = nc.dram_tensor("w1", [E, D, I_FF], F32, kind="ExternalInput")
    w3_h = nc.dram_tensor("w3", [E, D, I_FF], F32, kind="ExternalInput")
    w2_h = nc.dram_tensor("w2", [E, I_FF, D], F32, kind="ExternalInput")
    sw1_h = nc.dram_tensor("sw1", [D, 2816], F32, kind="ExternalInput")
    sw3_h = nc.dram_tensor("sw3", [D, 2816], F32, kind="ExternalInput")
    sw2_h = nc.dram_tensor("sw2", [2816, D], F32, kind="ExternalInput")
    ident_h = nc.dram_tensor("ident", [128, 128], F32, kind="ExternalInput").ap()
    maskT_h = nc.dram_tensor("maskT", [128, 128], F32, kind="ExternalInput").ap()
    y_out = nc.dram_tensor("y", [1024, D], F32, kind="ExternalOutput").ap()
    dbg_out = (nc.dram_tensor("dbg", [128, 8192], F32, kind="ExternalOutput").ap()
               if DBG is not None else None)

    from contextlib import ExitStack
    with ExitStack() as ST:
        EC = ST.enter_context
        R0 = EC(nc.sbuf_tensor("R0", [128, 8192], F32))
        R1 = EC(nc.sbuf_tensor("R1", [128, 8192], F32))
        R2 = EC(nc.sbuf_tensor("R2", [128, 8192], F32))
        RH = EC(nc.sbuf_tensor("RH", [128, 5632], F32))
        W = EC(nc.sbuf_tensor("W", [128, 8192], F32))
        WB = EC(nc.sbuf_tensor("WB", [128, 2048], F32))
        SIL = EC(nc.sbuf_tensor("SIL", [128, 512], F32))
        SST = EC(nc.sbuf_tensor("SST", [128, H * 128], F32))
        CID = EC(nc.sbuf_tensor("CID", [128, 128], F32))
        CMSK = EC(nc.sbuf_tensor("CMSK", [128, 128], F32))
        CF = EC(nc.sbuf_tensor("CF", [128, 512], F32))
        CB = EC(nc.sbuf_tensor("CB", [128, 8], F32))
        CGW = EC(nc.sbuf_tensor("CGW", [128, KT * E], F32))
        SC = EC(nc.sbuf_tensor("SC", [128, 64], F32))
        CEPS = EC(nc.sbuf_tensor("CEPS", [128, 1], F32))
        nc.const_aps.register(F32, EPS, CEPS[:, :])
        WD = EC(nc.sbuf_tensor("WD", [128, 4 * E], F32))
        PB = [EC(nc.psum_tensor(f"PB{i}", [128, 512], F32)) for i in range(8)]

        sem_pe = EC(nc.semaphore("s_pe"))
        sem_dve = EC(nc.semaphore("s_dve"))
        sem_act = EC(nc.semaphore("s_act"))
        sem_sp = EC(nc.semaphore("s_sp"))
        sem_gp = EC(nc.semaphore("s_gp"))

        P = Plan()
        P.sems = {"pe": sem_pe, "dve": sem_dve, "act": sem_act, "sp": sem_sp, "gp": sem_gp}

        bR0, bR1, bR2 = Buf("R0"), Buf("R1"), Buf("R2")
        bW = [Buf(f"Wq{i}") for i in range(4)]
        bWB = [Buf(f"WB{i}") for i in range(4)]
        bSIL = Buf("SIL")
        bS = [Buf(f"S{h}") for h in range(H)]
        bConst = Buf("const")
        bSC = Buf("scratch")
        bWD = Buf("wd")
        bP = [Buf(f"PB{i}") for i in range(8)]
        RHo = {"Kraw": 0, "T1": 512, "T2": 1024, "Kg": 1536, "KgT": 2048,
               "Vfm": 2560, "Vtm": 3072, "Qraw": 3584, "Qg": 4096,
               "sTs": 4608, "on": 4736, "sq": 4864, "oc": 4992}
        bRH = {k: Buf(k) for k in RHo}
        bXTM = Buf("xtm")
        bLNSQ = Buf("lnsq")
        bACT = Buf("act")
        RH_ATTN = list(bRH.values())
        RH_ALL = RH_ATTN + [bXTM, bLNSQ, bACT]

        rot_state = [0]

        def rot():
            i = rot_state[0]
            rot_state[0] = (i + 1) % 4
            return i

        rotq_state = [0]

        def rotq():
            i = rotq_state[0]
            rotq_state[0] = (i + 1) % 4
            return i

        def tap(tag, ap, reads):
            if DBG is None or tuple(tag) != tuple(DBG):
                return

            def fn(b, ap=ap):
                b.dma_start(out=dbg_out[:, 0:ap.shape[1]], in_=ap).then_inc(sem_gp, 16)
                return None
            P.step("gp", fn, reads=list(reads), dmas=1)

        def dap(handle, off, dims):
            return bass.AP(handle, off, dims)

        def ld_consts(b):
            b.dma_start(out=CID[:, :], in_=ident_h[:, :]).then_inc(sem_sp, 16)
            b.dma_start(out=CMSK[:, :], in_=maskT_h[:, :]).then_inc(sem_sp, 16)
            b.dma_start(out=CF[:, 0:128], in_=fq1T[:, :]).then_inc(sem_sp, 16)
            b.dma_start(out=CF[:, 128:256], in_=fq2T[:, :]).then_inc(sem_sp, 16)
            b.dma_start(out=CF[:, 256:384], in_=fk1T[:, :]).then_inc(sem_sp, 16)
            b.dma_start(out=CF[:, 384:512], in_=fk2T[:, :]).then_inc(sem_sp, 16)
            b.dma_start(out=CB[:, 0:1], in_=fq1b[:, :]).then_inc(sem_sp, 16)
            b.dma_start(out=CB[:, 1:2], in_=fq2b[:, :]).then_inc(sem_sp, 16)
            b.dma_start(out=CB[:, 2:3], in_=fk1b[:, :]).then_inc(sem_sp, 16)
            b.dma_start(out=CB[:, 3:4], in_=fk2b[:, :]).then_inc(sem_sp, 16)
            b.dma_start(out=CB[:, 4:5], in_=normw[:, :]).then_inc(sem_sp, 16)
            b.dma_start(out=dap(CGW, 0, [[KT * E, 128], [E, KT], [1, E]]),
                        in_=dap(gwT_h, 0, [[E, 128], [128 * E, KT], [1, E]])
                        ).then_inc(sem_sp, 16)
            return None
        P.step("sp", ld_consts, writes=[bConst], dmas=12)

        def s_zero(b):
            return b.memset(SST[:, :], 0.0)
        P.step("dve", s_zero, writes=list(bS))

        bEPS = Buf("eps")

        def eps_fill(b):
            return b.memset(CEPS[:, :], EPS)
        P.step("dve", eps_fill, writes=[bEPS])

        def load_colslice(handle, base_off, row_pitch, col0, q, bufs, ncols=128):
            def fn(b):
                for hf in range(2):
                    b.dma_start(
                        out=dap(W, q * 2048 + hf * 8 * ncols,
                                [[8192, 128], [ncols, 8], [1, ncols]]),
                        in_=dap(handle, base_off + hf * 8 * 128 * row_pitch + col0,
                                [[row_pitch, 128], [128 * row_pitch, 8], [1, ncols]]),
                    ).then_inc(sem_sp, 16)
                return None
            return P.step("sp", fn, writes=bufs, dmas=2)

        def proj_fm(q, bank, reads):
            def fn(b):
                for k in range(KT):
                    last = b.matmul(PB[bank][:, :],
                                    W[:, q * 2048 + k * 128:q * 2048 + (k + 1) * 128],
                                    R0[:, k * 512:(k + 1) * 512],
                                    start=(k == 0), stop=(k == KT - 1))
                return last
            return P.step("pe", fn, reads=reads, writes=[bP[bank]])

        for blk in range(NB):
            is_own = blk >= 2
            own_idx = blk - 2
            xcol = blk * TB

            def xload(b, xcol=xcol):
                for qtr in range(4):
                    b.dma_start(
                        out=dap(R0, qtr * 4 * 512, [[8192, 128], [512, 4], [1, 512]]),
                        in_=dap(xT_h, qtr * 4 * 128 * 2048 + xcol,
                                [[2048, 128], [128 * 2048, 4], [1, 512]]),
                    ).then_inc(sem_sp, 16)
                return None
            P.step("sp", xload, writes=[bR0], dmas=4)
            tap(("xT", blk), R0[:, :], [bR0])

            for h in range(H):
                head_conf = ([bACT, bXTM, bLNSQ] if (blk > 0 and h == 0) else [])
                qk = rotq()
                load_colslice(wkT_h, 0, D, h * 128, qk, [bW[qk]])
                qv = rotq()
                load_colslice(wvT_h, 0, D, h * 128, qv, [bW[qv]])
                if is_own:
                    qq = rotq()
                    load_colslice(wqT_h, 0, D, h * 128, qq, [bW[qq]])
                bank_k = rot()
                proj_fm(qk, bank_k, [bW[qk], bR0])
                bank_v = rot()
                proj_fm(qv, bank_v, [bW[qv], bR0])
                if is_own:
                    bank_q = rot()
                    proj_fm(qq, bank_q, [bW[qq], bR0])

                kraw = RH[:, RHo["Kraw"]:RHo["Kraw"] + 512]

                def evac_kraw(b, bank=bank_k, kraw=kraw):
                    return b.copy(kraw, PB[bank][:, :])
                P.step("act", evac_kraw, reads=[bP[bank_k]], writes=[bRH["Kraw"]],
                       conflicts=head_conf)

                def evac_vfm(b, bankv=bank_v):
                    return b.tensor_copy(RH[:, RHo["Vfm"]:RHo["Vfm"] + 512], PB[bankv][:, :])
                P.step("dve", evac_vfm, reads=[bP[bank_v]], writes=[bRH["Vfm"]],
                       conflicts=head_conf)

                if is_own:
                    qraw = RH[:, RHo["Qraw"]:RHo["Qraw"] + 512]

                    def evac_qraw(b, bankq=bank_q, qraw=qraw):
                        return b.copy(qraw, PB[bankq][:, :])
                    P.step("act", evac_qraw, reads=[bP[bank_q]], writes=[bRH["Qraw"]],
                           conflicts=head_conf)

                b1, b2 = rot(), rot()

                def gate_mm(b, b1=b1, b2=b2, kraw=kraw):
                    b.matmul(PB[b1][:, :], CF[:, 256:384], kraw, start=True, stop=True)
                    return b.matmul(PB[b2][:, :], CF[:, 384:512], kraw, start=True, stop=True)
                P.step("pe", gate_mm, reads=[bRH["Kraw"], bConst], writes=[bP[b1], bP[b2]])

                def evac_k1(b, b1=b1):
                    return b.activation(RH[:, RHo["T1"]:RHo["T1"] + 512], PB[b1][:, :],
                                        AF.Identity, bias=CB[:, 2:3], scale=1.0)
                P.step("act", evac_k1, reads=[bP[b1], bConst], writes=[bRH["T1"]],
                       conflicts=head_conf)

                def evac_k2(b, b2=b2):
                    return b.activation(RH[:, RHo["T2"]:RHo["T2"] + 512], PB[b2][:, :],
                                        AF.Identity, bias=CB[:, 3:4], scale=1.0)
                P.step("act", evac_k2, reads=[bP[b2], bConst], writes=[bRH["T2"]],
                       conflicts=head_conf)

                def kg_mul(b):
                    return b.tensor_mul(RH[:, RHo["Kg"]:RHo["Kg"] + 512],
                                        RH[:, RHo["T1"]:RHo["T1"] + 512],
                                        RH[:, RHo["T2"]:RHo["T2"] + 512])
                P.step("dve", kg_mul, reads=[bRH["T1"], bRH["T2"]], writes=[bRH["Kg"]],
                       conflicts=head_conf)

                if is_own:
                    bq1, bq2 = rot(), rot()

                    def qgate_mm(b, bq1=bq1, bq2=bq2, qraw=qraw):
                        b.matmul(PB[bq1][:, :], CF[:, 0:128], qraw, start=True, stop=True)
                        return b.matmul(PB[bq2][:, :], CF[:, 128:256], qraw, start=True, stop=True)
                    P.step("pe", qgate_mm, reads=[bRH["Qraw"], bConst],
                           writes=[bP[bq1], bP[bq2]])

                    def evac_q1(b, bq1=bq1):
                        return b.activation(RH[:, RHo["T1"]:RHo["T1"] + 512], PB[bq1][:, :],
                                            AF.Identity, bias=CB[:, 0:1], scale=1.0)
                    P.step("act", evac_q1, reads=[bP[bq1], bConst], writes=[bRH["T1"]])

                    def evac_q2(b, bq2=bq2):
                        return b.activation(RH[:, RHo["T2"]:RHo["T2"] + 512], PB[bq2][:, :],
                                            AF.Identity, bias=CB[:, 1:2], scale=1.0)
                    P.step("act", evac_q2, reads=[bP[bq2], bConst], writes=[bRH["T2"]])

                    def qg_mul(b):
                        return b.scalar_tensor_tensor(RH[:, RHo["Qg"]:RHo["Qg"] + 512],
                                                      RH[:, RHo["T1"]:RHo["T1"] + 512], SCALE,
                                                      RH[:, RHo["T2"]:RHo["T2"] + 512],
                                                      ALU.mult, ALU.mult)
                    P.step("dve", qg_mul, reads=[bRH["T1"], bRH["T2"]], writes=[bRH["Qg"]],
                           conflicts=head_conf)

                def kg_tr(b):
                    for c in range(CPB):
                        last = b.transpose(PB[6][:, c * 128:(c + 1) * 128],
                                           RH[:, RHo["Kg"] + c * 128:RHo["Kg"] + (c + 1) * 128],
                                           CID[:, :])
                    return last
                P.step("pe", kg_tr, reads=[bRH["Kg"], bConst], writes=[bP[6]])

                def kgt_evac(b):
                    return b.tensor_copy(RH[:, RHo["KgT"]:RHo["KgT"] + 512], PB[6][:, :])
                P.step("dve", kgt_evac, reads=[bP[6]], writes=[bRH["KgT"]],
                       conflicts=head_conf)

                def v_tr(b):
                    for c in range(CPB):
                        last = b.transpose(PB[6][:, c * 128:(c + 1) * 128],
                                           RH[:, RHo["Vfm"] + c * 128:RHo["Vfm"] + (c + 1) * 128],
                                           CID[:, :])
                    return last
                P.step("pe", v_tr, reads=[bRH["Vfm"], bConst], writes=[bP[6]])

                def vt_evac(b):
                    return b.tensor_copy(RH[:, RHo["Vtm"]:RHo["Vtm"] + 512], PB[6][:, :])
                P.step("dve", vt_evac, reads=[bP[6]], writes=[bRH["Vtm"]],
                       conflicts=head_conf)

                if not is_own:
                    def s_mm(b):
                        for c in range(CPB):
                            last = b.matmul(PB[7][:, 0:128],
                                            RH[:, RHo["KgT"] + c * 128:RHo["KgT"] + (c + 1) * 128],
                                            RH[:, RHo["Vtm"] + c * 128:RHo["Vtm"] + (c + 1) * 128],
                                            start=(c == 0), stop=(c == CPB - 1))
                        return last
                    P.step("pe", s_mm, reads=[bRH["KgT"], bRH["Vtm"]], writes=[bP[7]])

                    def s_add(b, h=h):
                        return b.tensor_add(SST[:, h * 128:(h + 1) * 128],
                                            SST[:, h * 128:(h + 1) * 128], PB[7][:, 0:128])
                    P.step("dve", s_add, reads=[bP[7]], writes=[bS[h]])
                    tap(("S", blk, h), SST[:, :], [bS[h]])
                    continue

                def mk_st(c):
                    kg_c = RH[:, RHo["Kg"] + c * 128:RHo["Kg"] + (c + 1) * 128]
                    qg_c = RH[:, RHo["Qg"] + c * 128:RHo["Qg"] + (c + 1) * 128]

                    def st_mm(b, kg_c=kg_c, qg_c=qg_c):
                        return b.matmul(PB[4][:, 0:128], kg_c, qg_c, start=True, stop=True)
                    P.step("pe", st_mm, reads=[bRH["Kg"], bRH["Qg"]], writes=[bP[4]])

                    def st_mask(b):
                        return b.tensor_mul(RH[:, RHo["sTs"]:RHo["sTs"] + 128],
                                            PB[4][:, 0:128], CMSK[:, :])
                    P.step("dve", st_mask, reads=[bP[4], bConst], writes=[bRH["sTs"]])

                def mk_tr(c, h):
                    def otr_f(b, c=c):
                        return b.transpose(PB[6][:, c * 128:(c + 1) * 128],
                                           RH[:, RHo["on"]:RHo["on"] + 128], CID[:, :])
                    P.step("pe", otr_f, reads=[bRH["on"], bConst], writes=[bP[6]])

                    def ofm_f(b, h=h, c=c):
                        return b.tensor_scalar_mul(
                            R1[:, h * 512 + c * 128:h * 512 + (c + 1) * 128],
                            PB[6][:, c * 128:(c + 1) * 128], CB[:, 4:5])
                    P.step("dve", ofm_f, reads=[bP[6], bConst], writes=[bR1])

                mk_st(0)
                for c in range(CPB):
                    kgt_c = RH[:, RHo["KgT"] + c * 128:RHo["KgT"] + (c + 1) * 128]
                    vt_c = RH[:, RHo["Vtm"] + c * 128:RHo["Vtm"] + (c + 1) * 128]
                    qg_c = RH[:, RHo["Qg"] + c * 128:RHo["Qg"] + (c + 1) * 128]
                    sts = RH[:, RHo["sTs"]:RHo["sTs"] + 128]
                    oc_ = RH[:, RHo["oc"]:RHo["oc"] + 128]
                    on_ = RH[:, RHo["on"]:RHo["on"] + 128]
                    sq_ = RH[:, RHo["sq"]:RHo["sq"] + 128]

                    def o_mm(b, qg_c=qg_c, sts=sts, vt_c=vt_c, h=h):
                        b.matmul(PB[5][:, 0:128], qg_c, SST[:, h * 128:(h + 1) * 128],
                                 start=True, stop=False)
                        return b.matmul(PB[5][:, 0:128], sts, vt_c, start=False, stop=True)
                    P.step("pe", o_mm, reads=[bRH["Qg"], bS[h], bRH["sTs"], bRH["Vtm"]],
                           writes=[bP[5]])

                    def kv_mm(b, kgt_c=kgt_c, vt_c=vt_c):
                        return b.matmul(PB[7][:, 0:128], kgt_c, vt_c, start=True, stop=True)
                    P.step("pe", kv_mm, reads=[bRH["KgT"], bRH["Vtm"]], writes=[bP[7]])

                    def oc_f(b, oc_=oc_):
                        return b.tensor_copy(oc_, PB[5][:, 0:128])
                    P.step("dve", oc_f, reads=[bP[5]], writes=[bRH["oc"]])

                    def sadd_f(b, h=h):
                        return b.tensor_add(SST[:, h * 128:(h + 1) * 128],
                                            SST[:, h * 128:(h + 1) * 128], PB[7][:, 0:128])
                    P.step("dve", sadd_f, reads=[bP[7]], writes=[bS[h]])

                    if c < CPB - 1:
                        mk_st(c + 1)
                    if c > 0:
                        mk_tr(c - 1, h)

                    def sq_f(b, sq_=sq_, oc_=oc_):
                        return b.activation(sq_, oc_, AF.Square)
                    P.step("act", sq_f, reads=[bRH["oc"]], writes=[bRH["sq"]])

                    def ss_f(b, sq_=sq_):
                        return b.reduce_sum(SC[:, 0:1], sq_, axis=AX)
                    P.step("dve", ss_f, reads=[bRH["sq"]], writes=[bSC])

                    def sd_f(b):
                        return b.activation(SC[:, 1:2], SC[:, 0:1], AF.Sqrt,
                                            bias=EPS, scale=1.0 / DK)
                    P.step("act", sd_f, reads=[bSC, bEPS], writes=[bSC])

                    def inv_f(b):
                        return b.reciprocal(SC[:, 2:3], SC[:, 1:2])
                    P.step("dve", inv_f, reads=[bSC], writes=[bSC])

                    def on_f(b, on_=on_, oc_=oc_):
                        return b.tensor_scalar_mul(on_, oc_, SC[:, 2:3])
                    P.step("dve", on_f, reads=[bRH["oc"], bSC], writes=[bRH["on"]])

                mk_tr(CPB - 1, h)

            if not is_own:
                continue

            for ds in range(8):
                qh = ds % 2

                def wo_load(b, ds=ds, qh=qh):
                    for hf in range(4):
                        b.dma_start(
                            out=dap(W, qh * 4096 + hf * 4 * 256,
                                    [[8192, 128], [256, 4], [1, 256]]),
                            in_=dap(woT_h, hf * 4 * 128 * D + ds * 256,
                                    [[D, 128], [128 * D, 4], [1, 256]]),
                        ).then_inc(sem_sp, 16)
                    return None
                P.step("sp", wo_load, writes=[bW[qh * 2], bW[qh * 2 + 1]], dmas=4)

                for t in range(CPB):
                    bank = rot()

                    def wo_mm(b, t=t, qh=qh, bank=bank):
                        for k in range(KT):
                            last = b.matmul(PB[bank][:, 0:256],
                                            R1[:, k * 512 + t * 128:k * 512 + (t + 1) * 128],
                                            W[:, qh * 4096 + k * 256:qh * 4096 + (k + 1) * 256],
                                            start=(k == 0), stop=(k == KT - 1))
                        return last
                    P.step("pe", wo_mm, reads=[bR1, bW[qh * 2], bW[qh * 2 + 1]],
                           writes=[bP[bank]])

                    def wo_evac(b, t=t, ds=ds, bank=bank):
                        return b.tensor_copy(
                            R2[:, t * 2048 + ds * 256:t * 2048 + (ds + 1) * 256],
                            PB[bank][:, 0:256])
                    P.step("dve", wo_evac, reads=[bP[bank]], writes=[bR2])

            tap(("r1pre", blk), R2[:, :], [bR2])
            for t in range(CPB):
                for k4 in range(4):
                    def xtr4(b, t=t, k4=k4):
                        for j in range(4):
                            k = k4 * 4 + j
                            last = b.transpose(PB[6][:, j * 128:(j + 1) * 128],
                                               R0[:, k * 512 + t * 128:k * 512 + (t + 1) * 128],
                                               CID[:, :])
                        return last
                    P.step("pe", xtr4, reads=[bR0, bConst], writes=[bP[6]])

                    def xtr4_evac(b, k4=k4):
                        return b.tensor_copy(RH[:, k4 * 512:(k4 + 1) * 512], PB[6][:, :])
                    P.step("dve", xtr4_evac, reads=[bP[6]], writes=[bXTM],
                           conflicts=(RH_ATTN if (t == 0 and k4 == 0) else []))

                def res_add(b, t=t):
                    return b.tensor_add(R2[:, t * 2048:(t + 1) * 2048],
                                        R2[:, t * 2048:(t + 1) * 2048],
                                        RH[:, 0:2048])
                P.step("dve", res_add, reads=[bXTM], writes=[bR2])

            tap(("r1", blk), R2[:, :], [bR2])
            for t in range(CPB):
                tile = R2[:, t * 2048:(t + 1) * 2048]

                def ln_nm(b, tile=tile, t=t):
                    return b.reduce_sum(SC[:, 8 + t:9 + t], tile, axis=AX, negate=True)
                P.step("dve", ln_nm, reads=[bR2], writes=[bSC])

                def ln_sq(b, tile=tile):
                    return b.activation(RH[:, 2048:4096], tile, AF.Square)
                P.step("act", ln_sq, reads=[bR2], writes=[bLNSQ],
                       conflicts=(RH_ATTN + [bXTM] if t == 0 else []))

                def ln_ssq(b, t=t):
                    return b.reduce_sum(SC[:, 12 + t:13 + t], RH[:, 2048:4096], axis=AX)
                P.step("dve", ln_ssq, reads=[bLNSQ], writes=[bSC])

            def ln_nmean(b):
                return b.activation(SC[:, 16:20], SC[:, 8:12], AF.Identity, scale=1.0 / D)
            P.step("act", ln_nmean, reads=[bSC], writes=[bSC])

            def ln_ex2(b):
                return b.activation(SC[:, 20:24], SC[:, 12:16], AF.Identity, scale=1.0 / D)
            P.step("act", ln_ex2, reads=[bSC], writes=[bSC])

            def ln_m2(b):
                return b.tensor_mul(SC[:, 24:28], SC[:, 16:20], SC[:, 16:20])
            P.step("dve", ln_m2, reads=[bSC], writes=[bSC])

            def ln_var(b):
                return b.tensor_sub(SC[:, 28:32], SC[:, 20:24], SC[:, 24:28])
            P.step("dve", ln_var, reads=[bSC], writes=[bSC])

            def ln_weps(b):
                return b.tensor_scalar_add(SC[:, 40:44], SC[:, 28:32], EPS)
            P.step("dve", ln_weps, reads=[bSC], writes=[bSC])

            def ln_sd(b):
                return b.activation(SC[:, 32:36], SC[:, 40:44], AF.Sqrt, bias=0.0, scale=1.0)
            P.step("act", ln_sd, reads=[bSC], writes=[bSC])

            def ln_rstd(b):
                return b.reciprocal(SC[:, 36:40], SC[:, 32:36])
            P.step("dve", ln_rstd, reads=[bSC], writes=[bSC])

            def ln_n1(b):
                return b.tensor_mul(SC[:, 44:48], SC[:, 36:40], SC[:, 36:40])
            P.step("dve", ln_n1, reads=[bSC], writes=[bSC])

            def ln_n2(b):
                return b.tensor_mul(SC[:, 32:36], SC[:, 44:48], SC[:, 40:44])
            P.step("dve", ln_n2, reads=[bSC], writes=[bSC])

            def ln_n3(b):
                return b.tensor_scalar(SC[:, 44:48], SC[:, 32:36], -0.5, 1.5,
                                       ALU.mult, ALU.add)
            P.step("dve", ln_n3, reads=[bSC], writes=[bSC])

            def ln_n4(b):
                return b.tensor_mul(SC[:, 36:40], SC[:, 36:40], SC[:, 44:48])
            P.step("dve", ln_n4, reads=[bSC], writes=[bSC])

            tap(("ln1sc", blk), SC[:, :], [bSC])
            for t in range(CPB):
                tile = R2[:, t * 2048:(t + 1) * 2048]

                def ln_c(b, tile=tile, t=t):
                    return b.tensor_scalar_add(tile, tile, SC[:, 16 + t:17 + t])
                P.step("dve", ln_c, reads=[bSC], writes=[bR2])

                def ln_s(b, tile=tile, t=t):
                    return b.tensor_scalar_mul(tile, tile, SC[:, 36 + t:37 + t])
                P.step("dve", ln_s, reads=[bSC], writes=[bR2])

            tap(("y1", blk), R2[:, :], [bR2])
            for t in range(CPB):
                for k4 in range(4):
                    def ytr4(b, t=t, k4=k4):
                        for j in range(4):
                            k = k4 * 4 + j
                            last = b.transpose(PB[6][:, j * 128:(j + 1) * 128],
                                               R2[:, t * 2048 + k * 128:t * 2048 + (k + 1) * 128],
                                               CID[:, :])
                        return last
                    P.step("pe", ytr4, reads=[bR2, bConst], writes=[bP[6]])

                    def ytr4_evac(b, t=t, k4=k4):
                        return b.tensor_copy(
                            dap(R1, k4 * 4 * 512 + t * 128, [[8192, 128], [512, 4], [1, 128]]),
                            dap(PB[6], 0, [[512, 128], [128, 4], [1, 128]]))
                    P.step("dve", ytr4_evac, reads=[bP[6]], writes=[bR1])

            tap(("y1T", blk), R1[:, :], [bR1])
            for t in range(CPB):
                bank = rot()

                def lg_mm(b, t=t, bank=bank):
                    for k in range(KT):
                        last = b.matmul(PB[bank][:, 0:E],
                                        R1[:, k * 512 + t * 128:k * 512 + (t + 1) * 128],
                                        CGW[:, k * E:(k + 1) * E],
                                        start=(k == 0), stop=(k == KT - 1))
                    return last
                P.step("pe", lg_mm, reads=[bR1, bConst], writes=[bP[bank]])

                def rt_max(b, bank=bank):
                    return b.reduce_max(SC[:, 0:1], PB[bank][:, 0:E], axis=AX)
                P.step("dve", rt_max, reads=[bP[bank]], writes=[bSC])

                def rt_neg(b):
                    return b.tensor_scalar_mul(SC[:, 1:2], SC[:, 0:1], -1.0)
                P.step("dve", rt_neg, reads=[bSC], writes=[bSC])

                def rt_exp(b, bank=bank):
                    return b.activation(SC[:, 48:56], PB[bank][:, 0:E], AF.Exp,
                                        bias=SC[:, 1:2], scale=1.0)
                P.step("act", rt_exp, reads=[bP[bank], bSC], writes=[bSC])

                def rt_m1(b):
                    return b.reduce_max(SC[:, 2:3], SC[:, 48:56], axis=AX)
                P.step("dve", rt_m1, reads=[bSC], writes=[bSC])

                def rt_t1(b):
                    return b.tensor_single_scalar(SC[:, 56:64], SC[:, 48:56],
                                                  SC[:, 2:3], ALU.is_ge)
                P.step("dve", rt_t1, reads=[bSC], writes=[bSC])

                def rt_mask(b):
                    return b.scalar_tensor_tensor(SC[:, 56:64], SC[:, 56:64], -10.0,
                                                  SC[:, 48:56], ALU.mult, ALU.add)
                P.step("dve", rt_mask, reads=[bSC], writes=[bSC])

                def rt_m2(b):
                    return b.reduce_max(SC[:, 3:4], SC[:, 56:64], axis=AX)
                P.step("dve", rt_m2, reads=[bSC], writes=[bSC])

                def rt_ge(b):
                    return b.tensor_single_scalar(SC[:, 56:64], SC[:, 48:56],
                                                  SC[:, 3:4], ALU.is_ge)
                P.step("dve", rt_ge, reads=[bSC], writes=[bSC])

                def rt_w1(b):
                    return b.tensor_mul(SC[:, 48:56], SC[:, 48:56], SC[:, 56:64])
                P.step("dve", rt_w1, reads=[bSC], writes=[bSC])

                def rt_den(b):
                    return b.reduce_sum(SC[:, 4:5], SC[:, 48:56], axis=AX)
                P.step("dve", rt_den, reads=[bSC], writes=[bSC])

                def rt_rec(b):
                    return b.reciprocal(SC[:, 5:6], SC[:, 4:5])
                P.step("dve", rt_rec, reads=[bSC], writes=[bSC])

                if t == 0:
                    tap(("rtsc", blk), SC[:, :], [bSC])

                def rt_w2(b, t=t):
                    return b.tensor_scalar_mul(WD[:, t * E:(t + 1) * E],
                                               SC[:, 48:56], SC[:, 5:6])
                P.step("dve", rt_w2, reads=[bSC], writes=[bWD])

            def ffn_hg(h1, h3, n_i, i_base, e_off, row_pitch, first_conf):
                for i in range(n_i):
                    q1 = (2 * i) % 4
                    q3 = (2 * i + 1) % 4
                    load_colslice(h1, e_off, row_pitch, (i_base + i) * 128, q1, [bW[q1]])
                    bh = rot()

                    def h_mm(b, q1=q1, bh=bh):
                        for k in range(KT):
                            last = b.matmul(PB[bh][:, :],
                                            W[:, q1 * 2048 + k * 128:q1 * 2048 + (k + 1) * 128],
                                            R1[:, k * 512:(k + 1) * 512],
                                            start=(k == 0), stop=(k == KT - 1))
                        return last
                    P.step("pe", h_mm, reads=[bW[q1], bR1], writes=[bP[bh]])

                    def silu_f(b, bh=bh):
                        return b.activation(SIL[:, :], PB[bh][:, :], AF.Silu)
                    P.step("act", silu_f, reads=[bP[bh]], writes=[bSIL])

                    load_colslice(h3, e_off, row_pitch, (i_base + i) * 128, q3, [bW[q3]])
                    bg = rot()

                    def g_mm(b, q3=q3, bg=bg):
                        for k in range(KT):
                            last = b.matmul(PB[bg][:, :],
                                            W[:, q3 * 2048 + k * 128:q3 * 2048 + (k + 1) * 128],
                                            R1[:, k * 512:(k + 1) * 512],
                                            start=(k == 0), stop=(k == KT - 1))
                        return last
                    P.step("pe", g_mm, reads=[bW[q3], bR1], writes=[bP[bg]])

                    def act_mul(b, i=i, bg=bg):
                        return b.tensor_mul(RH[:, i * 512:(i + 1) * 512], SIL[:, :],
                                            PB[bg][:, :])
                    P.step("dve", act_mul, reads=[bSIL, bP[bg]], writes=[bACT],
                           conflicts=(first_conf if i == 0 else []))

            def ffn_y(h2, n_i, i_base, e_off2, evac_kind, e_idx=None):
                for ds in range(4):
                    for i in range(n_i):
                        wslot = i % 4

                        def lb(b, i=i, ds=ds, wslot=wslot, e_off2=e_off2,
                               i_base=i_base, h2=h2):
                            b.dma_start(
                                out=WB[:, wslot * 512:(wslot + 1) * 512],
                                in_=dap(h2, e_off2 + (i_base + i) * 128 * D + ds * 512,
                                        [[D, 128], [1, 512]]),
                            ).then_inc(sem_sp, 16)
                            return None
                        P.step("sp", lb, writes=[bWB[wslot]], dmas=1)

                        def y_mm(b, i=i, n_i=n_i, wslot=wslot):
                            for t in range(CPB):
                                last = b.matmul(PB[4 + t][:, :],
                                                RH[:, i * 512 + t * 128:i * 512 + (t + 1) * 128],
                                                WB[:, wslot * 512:(wslot + 1) * 512],
                                                start=(i == 0), stop=(i == n_i - 1))
                            return last
                        P.step("pe", y_mm, reads=[bACT, bWB[wslot]],
                               writes=[bP[4], bP[5], bP[6], bP[7]])

                    for t in range(CPB):
                        if evac_kind == "shared0":
                            def ev(b, t=t, ds=ds):
                                return b.tensor_add(
                                    R0[:, t * 2048 + ds * 512:t * 2048 + (ds + 1) * 512],
                                    PB[4 + t][:, :],
                                    R2[:, t * 2048 + ds * 512:t * 2048 + (ds + 1) * 512])
                            P.step("dve", ev, reads=[bP[4 + t], bR2], writes=[bR0])
                        elif evac_kind == "shared1":
                            def ev(b, t=t, ds=ds):
                                return b.tensor_add(
                                    R0[:, t * 2048 + ds * 512:t * 2048 + (ds + 1) * 512],
                                    R0[:, t * 2048 + ds * 512:t * 2048 + (ds + 1) * 512],
                                    PB[4 + t][:, :])
                            P.step("dve", ev, reads=[bP[4 + t]], writes=[bR0])
                        else:
                            def ev(b, t=t, ds=ds, e_idx=e_idx):
                                return b.scalar_tensor_tensor(
                                    R0[:, t * 2048 + ds * 512:t * 2048 + (ds + 1) * 512],
                                    PB[4 + t][:, :],
                                    WD[:, t * E + e_idx:t * E + e_idx + 1],
                                    R0[:, t * 2048 + ds * 512:t * 2048 + (ds + 1) * 512],
                                    ALU.mult, ALU.add)
                            P.step("dve", ev, reads=[bP[4 + t], bWD], writes=[bR0])

            ffn_conf = [bXTM, bLNSQ] + RH_ATTN
            ffn_hg(sw1_h, sw3_h, 11, 0, 0, 2816, ffn_conf)
            ffn_y(sw2_h, 11, 0, 0, "shared0")
            tap(("yacc0", blk), R0[:, :], [bR0])
            ffn_hg(sw1_h, sw3_h, 11, 11, 0, 2816, [])
            ffn_y(sw2_h, 11, 11, 0, "shared1")
            tap(("yacc1", blk), R0[:, :], [bR0])
            for e in range(E):
                ffn_hg(w1_h, w3_h, I_T, 0, e * D * I_FF, I_FF, [])
                ffn_y(w2_h, I_T, 0, e * I_FF * D, "expert", e_idx=e)
                tap(("yacc_e", blk, e), R0[:, :], [bR0])

            for t in range(CPB):
                tile = R0[:, t * 2048:(t + 1) * 2048]

                def l2_nm(b, tile=tile, t=t):
                    return b.reduce_sum(SC[:, 8 + t:9 + t], tile, axis=AX, negate=True)
                P.step("dve", l2_nm, reads=[bR0], writes=[bSC])

                def l2_sq(b, tile=tile):
                    return b.activation(RH[:, 0:2048], tile, AF.Square)
                P.step("act", l2_sq, reads=[bR0], writes=[bXTM],
                       conflicts=([bACT, bLNSQ] if t == 0 else []))

                def l2_ssq(b, t=t):
                    return b.reduce_sum(SC[:, 12 + t:13 + t], RH[:, 0:2048], axis=AX)
                P.step("dve", l2_ssq, reads=[bXTM], writes=[bSC])

            def l2_nmean(b):
                return b.activation(SC[:, 16:20], SC[:, 8:12], AF.Identity, scale=1.0 / D)
            P.step("act", l2_nmean, reads=[bSC], writes=[bSC])

            def l2_ex2(b):
                return b.activation(SC[:, 20:24], SC[:, 12:16], AF.Identity, scale=1.0 / D)
            P.step("act", l2_ex2, reads=[bSC], writes=[bSC])

            def l2_m2(b):
                return b.tensor_mul(SC[:, 24:28], SC[:, 16:20], SC[:, 16:20])
            P.step("dve", l2_m2, reads=[bSC], writes=[bSC])

            def l2_var(b):
                return b.tensor_sub(SC[:, 28:32], SC[:, 20:24], SC[:, 24:28])
            P.step("dve", l2_var, reads=[bSC], writes=[bSC])

            def l2_weps(b):
                return b.tensor_scalar_add(SC[:, 40:44], SC[:, 28:32], EPS)
            P.step("dve", l2_weps, reads=[bSC], writes=[bSC])

            def l2_sd(b):
                return b.activation(SC[:, 32:36], SC[:, 40:44], AF.Sqrt, bias=0.0, scale=1.0)
            P.step("act", l2_sd, reads=[bSC], writes=[bSC])

            def l2_rstd(b):
                return b.reciprocal(SC[:, 36:40], SC[:, 32:36])
            P.step("dve", l2_rstd, reads=[bSC], writes=[bSC])

            def l2_n1(b):
                return b.tensor_mul(SC[:, 44:48], SC[:, 36:40], SC[:, 36:40])
            P.step("dve", l2_n1, reads=[bSC], writes=[bSC])

            def l2_n2(b):
                return b.tensor_mul(SC[:, 32:36], SC[:, 44:48], SC[:, 40:44])
            P.step("dve", l2_n2, reads=[bSC], writes=[bSC])

            def l2_n3(b):
                return b.tensor_scalar(SC[:, 44:48], SC[:, 32:36], -0.5, 1.5,
                                       ALU.mult, ALU.add)
            P.step("dve", l2_n3, reads=[bSC], writes=[bSC])

            def l2_n4(b):
                return b.tensor_mul(SC[:, 36:40], SC[:, 36:40], SC[:, 44:48])
            P.step("dve", l2_n4, reads=[bSC], writes=[bSC])

            for t in range(CPB):
                tile = R0[:, t * 2048:(t + 1) * 2048]

                def l2_c(b, tile=tile, t=t):
                    return b.tensor_scalar_add(tile, tile, SC[:, 16 + t:17 + t])
                P.step("dve", l2_c, reads=[bSC], writes=[bR0])

                def l2_s(b, tile=tile, t=t):
                    return b.tensor_scalar_mul(tile, tile, SC[:, 36 + t:37 + t])
                P.step("dve", l2_s, reads=[bSC], writes=[bR0])

            for t in range(CPB):
                def out_dma(b, t=t, own_idx=own_idx):
                    b.dma_start(out=y_out[own_idx * TB + t * 128:own_idx * TB + (t + 1) * 128, :],
                                in_=R0[:, t * 2048:(t + 1) * 2048]).then_inc(sem_gp, 16)
                    return None
                P.step("gp", out_dma, reads=[bR0], dmas=1)

        global LAST_PLAN
        LAST_PLAN = P
        with nc.Block() as block:
            @block.sync
            def _(b):
                P.run_engine("sp", b)

            @block.gpsimd
            def _(b):
                P.run_engine("gp", b)

            @block.tensor
            def _(b):
                P.run_engine("pe", b)

            @block.vector
            def _(b):
                P.run_engine("dve", b)

            @block.scalar
            def _(b):
                P.run_engine("act", b)

    return nc


def make_in_maps(inputs):
    x = np.asarray(inputs["L0_attn_x"], np.float32)

    def f32c(a):
        return np.ascontiguousarray(np.asarray(a, np.float32))

    shared = {
        "wqT": f32c(np.asarray(inputs["L0_attn_wq"]).T),
        "wkT": f32c(np.asarray(inputs["L0_attn_wk"]).T),
        "wvT": f32c(np.asarray(inputs["L0_attn_wv"]).T),
        "woT": f32c(np.asarray(inputs["L0_attn_wo"]).T),
        "fq1T": f32c(np.asarray(inputs["L0_attn_fq1_w"]).T),
        "fq2T": f32c(np.asarray(inputs["L0_attn_fq2_w"]).T),
        "fk1T": f32c(np.asarray(inputs["L0_attn_fk1_w"]).T),
        "fk2T": f32c(np.asarray(inputs["L0_attn_fk2_w"]).T),
        "fq1b": f32c(np.asarray(inputs["L0_attn_fq1_b"]).reshape(DK, 1)),
        "fq2b": f32c(np.asarray(inputs["L0_attn_fq2_b"]).reshape(DK, 1)),
        "fk1b": f32c(np.asarray(inputs["L0_attn_fk1_b"]).reshape(DK, 1)),
        "fk2b": f32c(np.asarray(inputs["L0_attn_fk2_b"]).reshape(DK, 1)),
        "normw": f32c(np.asarray(inputs["L0_attn_norm_w"]).reshape(DK, 1)),
        "gwT": f32c(np.asarray(inputs["L0_ffn_gate_w"]).T),
        "w1": f32c(inputs["L0_ffn_w1"]),
        "w3": f32c(inputs["L0_ffn_w3"]),
        "w2": f32c(inputs["L0_ffn_w2"]),
        "sw1": f32c(inputs["L0_ffn_sw1"]),
        "sw3": f32c(inputs["L0_ffn_sw3"]),
        "sw2": f32c(inputs["L0_ffn_sw2"]),
        "ident": np.eye(128, dtype=np.float32),
        "maskT": np.triu(np.ones((128, 128), dtype=np.float32)),
    }
    in_maps = []
    for core in range(8):
        b, half = core // 2, core % 2
        own = x[b, half * 1024:(half + 1) * 1024, :]
        if half == 0:
            pref_T = np.zeros((D, 1024), np.float32)
        else:
            pref_T = x[b, 0:1024, :].T
        m = dict(shared)
        m["xT"] = f32c(np.concatenate([pref_T, own.T], axis=1))
        in_maps.append(m)
    return in_maps


def kernel(**inputs):
    nc = build_program()
    in_maps = make_in_maps(inputs)
    res = run_bass_kernel_spmd(nc, in_maps, core_ids=list(range(8)))
    out = np.empty((4, 2048, 2048), np.float32)
    for core in range(8):
        b, half = core // 2, core % 2
        out[b, half * 1024:(half + 1) * 1024, :] = np.asarray(res.results[core]["y"])
    return out
